# version 01
# baseline (speedup 1.0000x reference)
import numpy as np
import ml_dtypes
import concourse.bass as bass
import concourse.mybir as mybir
from concourse.bass_utils import run_bass_kernel_spmd

F32 = mybir.dt.float32
BF16 = mybir.dt.bfloat16
AF = mybir.ActivationFunctionType
ALU = mybir.AluOpType
AX = mybir.AxisListType.X

H = 4096
D = 8192
N = 16
R = 256
KC = 4
OWN = 2048
EXT = 256
TOK = OWN + EXT
TC = 512
TS = 128
SEG = TS + 2
NDT = D // 128
NC_CORES = 8
DTG = 8
BIGNEG = -1.0e30
CHUNKS = [(0, EXT)] + [(EXT + i * TC, EXT + (i + 1) * TC) for i in range(4)]

CONSTS = [0.0, 1.0, -1.0, 0.5, -0.5, 2.0, 1e-6, 1e-5,
          1.0 / H, 1.0 / R, 1.0 / N, BIGNEG, -1e-5, 3.0]

CFG = dict(scan_bf16=False, pool_offload=True, debug=False,
           stop_after='full', nchunks=len(CHUNKS))


class Sched:
    ENG_SEM = {"pe": ("s_pe", 1), "act": ("s_act", 1), "dve": ("s_dve", 1),
               "gpc": ("s_gpc", 1), "hw": ("s_hw", 16), "sw": ("s_sw", 16)}
    ENG_QUEUE = {"pe": "tensor", "act": "scalar", "dve": "vector",
                 "gpc": "gpsimd", "hw": "sync", "sw": "gpsimd"}

    def __init__(self):
        self.ops = []
        self.counts = {s: 0 for s, _ in self.ENG_SEM.values()}
        self.writer = {}
        self.readers = {}

    def add(self, stream, fn, reads=(), writes=()):
        sem, amt = self.ENG_SEM[stream]
        if "cst" in self.writer and "cst" not in writes:
            reads = list(reads) + ["cst"]
        deps = {}

        def dep(s, v):
            if s == sem and stream == "pe":
                return
            if deps.get(s, 0) < v:
                deps[s] = v

        for b in reads:
            w = self.writer.get(b)
            if w:
                dep(*w)
        for b in writes:
            w = self.writer.get(b)
            if w:
                dep(*w)
            for r in self.readers.get(b, ()):
                dep(*r)
        self.counts[sem] += amt
        me = (sem, self.counts[sem])
        for b in reads:
            self.readers.setdefault(b, []).append(me)
        for b in writes:
            self.writer[b] = me
            self.readers[b] = []
        self.ops.append((stream, fn, deps, me, amt))

    def emit(self, block, sems):
        by_queue = {}
        for st, q in self.ENG_QUEUE.items():
            by_queue.setdefault(q, set()).add(st)

        def make_body(queue, sts):
            mine = [op for op in self.ops if op[0] in sts]

            def body(eng):
                hw = {}
                for (_st, fn, deps, (sem, _val), amt) in mine:
                    for s in sorted(deps):
                        v = deps[s]
                        if hw.get(s, 0) < v:
                            eng.wait_ge(sems[s], v)
                            hw[s] = v
                    fn(eng).then_inc(sems[sem], amt)
                if queue == "sync":
                    for s, cnt in self.counts.items():
                        if cnt > 0:
                            eng.wait_ge(sems[s], cnt)
            return body

        for q, sts in by_queue.items():
            getattr(block, q)(make_body(q, sts))


def _v(handle_or_ap, offset, dims):
    base = handle_or_ap.ap() if hasattr(handle_or_ap, "ap") else handle_or_ap
    return bass.AP(base.tensor, offset, [list(d) for d in dims])


def build_program(cfg):
    nc = bass.Bass("TRN2", target_bir_lowering=False, debug=False)
    sc_bf = cfg["scan_bf16"]
    SDT = BF16 if sc_bf else F32

    x_ext = nc.dram_tensor("x_ext", [TOK, H], F32, kind="ExternalInput")
    maskr = nc.dram_tensor("maskr", [1, TOK], F32, kind="ExternalInput")
    WinT = nc.dram_tensor("WinT", [H, 2 * D], F32, kind="ExternalInput")
    WxT = nc.dram_tensor("WxT", [D, 288], F32, kind="ExternalInput")
    WdtT = nc.dram_tensor("WdtT", [R, D], F32, kind="ExternalInput")
    WoT = nc.dram_tensor("WoT", [D, H], F32, kind="ExternalInput")
    AfT = nc.dram_tensor("AfT", [H, 1024], F32, kind="ExternalInput")
    BfT = nc.dram_tensor("BfT", [1024, H], F32, kind="ExternalInput")
    wr_l = nc.dram_tensor("wr_l", [128, 32 * 64], F32, kind="ExternalInput")
    A_l = nc.dram_tensor("A_l", [128, NDT * N], F32, kind="ExternalInput")
    cw_l = nc.dram_tensor("cw_l", [128, NDT * KC], F32, kind="ExternalInput")
    cb_l = nc.dram_tensor("cb_l", [128, NDT], F32, kind="ExternalInput")
    dtb_l = nc.dram_tensor("dtb_l", [128, NDT], F32, kind="ExternalInput")
    Dp_l = nc.dram_tensor("Dp_l", [128, NDT], F32, kind="ExternalInput")
    dtln_r = nc.dram_tensor("dtln_r", [128, R], F32, kind="ExternalInput")
    bln_r = nc.dram_tensor("bln_r", [128, N], F32, kind="ExternalInput")
    cln_r = nc.dram_tensor("cln_r", [128, N], F32, kind="ExternalInput")
    consts_h = nc.dram_tensor("consts_h", [128, len(CONSTS)], F32,
                              kind="ExternalInput")
    ident_b = nc.dram_tensor("ident_b", [128, 128], BF16, kind="ExternalInput")
    ident_f = nc.dram_tensor("ident_f", [128, 128], F32, kind="ExternalInput")
    sel_h = nc.dram_tensor("sel_h", [64, 8 * 128], BF16, kind="ExternalInput")

    out_own = nc.dram_tensor("out_own", [OWN, H], F32, kind="ExternalOutput")
    dbg = {}
    if cfg["debug"]:
        dbg["u"] = nc.dram_tensor("dbg_u", [D, TOK], BF16,
                                  kind="ExternalOutput")
        dbg["y"] = nc.dram_tensor("dbg_y", [D, OWN], BF16,
                                  kind="ExternalOutput")
        dbg["xn"] = nc.dram_tensor("dbg_xn", [TOK, H], BF16,
                                   kind="ExternalOutput")
        dbg["r1"] = nc.dram_tensor("dbg_r1", [OWN, H], F32,
                                   kind="ExternalOutput")
        dbg["del"] = nc.dram_tensor("dbg_del", [2 * D, TC], F32,
                                    kind="ExternalOutput")
        dbg["xn2"] = nc.dram_tensor("dbg_xn2", [OWN, H], BF16,
                                    kind="ExternalOutput")
        dbg["gbs"] = nc.dram_tensor("dbg_gbs", [128, 8 * 512], BF16,
                                    kind="ExternalOutput")
        dbg["hgs"] = nc.dram_tensor("dbg_hgs", [128, 8 * 512], BF16,
                                    kind="ExternalOutput")
        dbg["gt"] = nc.dram_tensor("dbg_gt", [64, 512], BF16,
                                   kind="ExternalOutput")
        dbg["sco"] = nc.dram_tensor("dbg_sco", [128, 64], F32,
                                    kind="ExternalOutput")
        dbg["tk"] = nc.dram_tensor("dbg_tk", [128, 320], F32,
                                   kind="ExternalOutput")

    xn_d = nc.dram_tensor("xn_d", [TOK, H], BF16)
    Wibf = nc.dram_tensor("Wibf", [H, 2 * D], BF16)
    Wobf = nc.dram_tensor("Wobf", [D, H], BF16)
    Afbf = nc.dram_tensor("Afbf", [H, 1024], BF16)
    Bfbf = nc.dram_tensor("Bfbf", [1024, H], BF16)
    Wxbf = nc.dram_tensor("Wxbf", [D, 288], BF16)
    Wdtbf = nc.dram_tensor("Wdtbf", [R, D], BF16)
    u_sp = nc.dram_tensor("u_sp", [D, TOK], BF16)
    g_sp = nc.dram_tensor("g_sp", [D, OWN], BF16)
    y_sp = nc.dram_tensor("y_sp", [D, OWN], BF16)
    del_d = nc.dram_tensor("del_d", [2 * D, TC], F32)
    du_d = nc.dram_tensor("du_d", [2 * D, TC], SDT)
    bc_d = nc.dram_tensor("bc_d", [2 * 64, TC], BF16)
    r1_d = nc.dram_tensor("r1_d", [OWN, H], F32)
    xn2_d = nc.dram_tensor("xn2_d", [OWN, H], BF16)

    S = Sched()

    import contextlib
    with contextlib.ExitStack() as _stk:
        cst = _stk.enter_context(nc.sbuf_tensor("cst", [128, len(CONSTS)], F32))
        A_sb = _stk.enter_context(nc.sbuf_tensor("A_sb", [128, NDT * N], F32))
        cw_sb = _stk.enter_context(nc.sbuf_tensor("cw_sb", [128, NDT * KC], F32))
        cb_sb = _stk.enter_context(nc.sbuf_tensor("cb_sb", [128, NDT], F32))
        dtb_sb = _stk.enter_context(nc.sbuf_tensor("dtb_sb", [128, NDT], F32))
        Dp_sb = _stk.enter_context(nc.sbuf_tensor("Dp_sb", [128, NDT], F32))
        dtln_sb = _stk.enter_context(nc.sbuf_tensor("dtln_sb", [128, R], F32))
        bln_sb = _stk.enter_context(nc.sbuf_tensor("bln_sb", [128, N], F32))
        cln_sb = _stk.enter_context(nc.sbuf_tensor("cln_sb", [128, N], F32))
        idb_sb = _stk.enter_context(nc.sbuf_tensor("idb_sb", [128, 128], BF16))
        idf_sb = _stk.enter_context(nc.sbuf_tensor("idf_sb", [128, 128], F32))
        sel_sb = _stk.enter_context(nc.sbuf_tensor("sel_sb", [64, 8 * 128], BF16))
        wr_sb = _stk.enter_context(nc.sbuf_tensor("wr_sb", [128, 32 * 64], BF16))
        mask_sb = _stk.enter_context(nc.sbuf_tensor("mask_sb", [128, TC], F32))
        xst = _stk.enter_context(nc.sbuf_tensor("xst", [128, 1024], F32))
        sqd = _stk.enter_context(nc.sbuf_tensor("sqd", [128, 1024], F32))
        lnt = _stk.enter_context(nc.sbuf_tensor("lnt", [128, 16], F32))
        xnst = _stk.enter_context(nc.sbuf_tensor("xnst", [128, 1024], BF16))
        xn_cm = _stk.enter_context(nc.sbuf_tensor("xn_cm", [128, 32 * TC], BF16))
        wsl = _stk.enter_context(nc.sbuf_tensor("wsl", [128, 2 * 16 * 128], BF16))
        uraw = _stk.enter_context(nc.sbuf_tensor("uraw", [128, TC + 3], BF16))
        cacc = _stk.enter_context(nc.sbuf_tensor("cacc", [128, TC], F32))
        upost = _stk.enter_context(nc.sbuf_tensor("upost", [128, TC], BF16))
        gstg = _stk.enter_context(nc.sbuf_tensor("gstg", [128, TC], BF16))
        halo = _stk.enter_context(nc.sbuf_tensor("halo", [128, 2 * NDT * 3], BF16))
        wxt_sb = _stk.enter_context(nc.sbuf_tensor("wxt_sb", [128, 288], BF16))
        Pcme = _stk.enter_context(nc.sbuf_tensor("Pcme", [128, TC], F32))
        Ptm = _stk.enter_context(nc.sbuf_tensor("Ptm", [128, 4 * 288], F32))
        sq2 = _stk.enter_context(nc.sbuf_tensor("sq2", [128, 288], F32))
        dtn = _stk.enter_context(nc.sbuf_tensor("dtn", [128, 4 * 256], BF16))
        dtnT = _stk.enter_context(nc.sbuf_tensor("dtnT", [128, 2 * TC], BF16))
        bcth = _stk.enter_context(nc.sbuf_tensor("bcth", [128, 32], BF16))
        bct = _stk.enter_context(nc.sbuf_tensor("bct", [64, TC], F32))
        wdt_sb = _stk.enter_context(nc.sbuf_tensor("wdt_sb", [128, 2 * 8 * 128], BF16))
        ztl = _stk.enter_context(nc.sbuf_tensor("ztl", [128, TC], F32))
        zab = _stk.enter_context(nc.sbuf_tensor("zab", [128, TC], F32))
        dutb = _stk.enter_context(nc.sbuf_tensor("dutb", [128, TC], SDT)) if sc_bf else None
        uslD = _stk.enter_context(nc.sbuf_tensor("uslD", [128, TC], BF16))
        bcB = _stk.enter_context(nc.sbuf_tensor("bcB", [128, N * TS], BF16))
        bcC = _stk.enter_context(nc.sbuf_tensor("bcC", [128, N * TS], BF16))
        dAt = _stk.enter_context(nc.sbuf_tensor("dAt", [128, N * SEG], SDT))
        bbt = _stk.enter_context(nc.sbuf_tensor("bbt", [128, N * SEG], SDT))
        stt = _stk.enter_context(nc.sbuf_tensor("stt", [128, N * SEG], SDT))
        dsl = _stk.enter_context(nc.sbuf_tensor("dsl", [128, DTG * TS], F32))
        dusl = _stk.enter_context(nc.sbuf_tensor("dusl", [128, DTG * TS], SDT))
        usl = _stk.enter_context(nc.sbuf_tensor("usl", [128, DTG * TS], BF16))
        gsl = _stk.enter_context(nc.sbuf_tensor("gsl", [128, DTG * TS], BF16))
        ysl = _stk.enter_context(nc.sbuf_tensor("ysl", [128, DTG * TS], BF16))
        ytmp = _stk.enter_context(nc.sbuf_tensor("ytmp", [128, TS], F32))
        z1t = _stk.enter_context(nc.sbuf_tensor("z1t", [128, TS], F32))
        sgt = _stk.enter_context(nc.sbuf_tensor("sgt", [128, TS], BF16))
        carry = _stk.enter_context(nc.sbuf_tensor("carry", [128, NDT * N], F32))
        wB1 = _stk.enter_context(nc.sbuf_tensor("wB1", [128, 4 * 512], BF16))
        ysb1 = _stk.enter_context(nc.sbuf_tensor("ysb1", [128, 4 * 512], BF16))
        xsl1 = _stk.enter_context(nc.sbuf_tensor("xsl1", [128, 4 * 512], F32))
        x2T = _stk.enter_context(nc.sbuf_tensor("x2T", [128, 2 * 512], BF16))
        afs = _stk.enter_context(nc.sbuf_tensor("afs", [128, 2 * 512], BF16))
        sco = _stk.enter_context(nc.sbuf_tensor("sco", [128, 64], F32))
        scw = _stk.enter_context(nc.sbuf_tensor("scw", [128, 64], F32))
        mcol = _stk.enter_context(nc.sbuf_tensor("mcol", [128, 4], F32))
        gels = _stk.enter_context(nc.sbuf_tensor("gels", [128, 64], F32))
        Gf = _stk.enter_context(nc.sbuf_tensor("Gf", [128, 64], F32))
        Gb16 = _stk.enter_context(nc.sbuf_tensor("Gb16", [128, 64], BF16))
        GT = _stk.enter_context(nc.sbuf_tensor("GT", [64, 512], BF16))
        Gbs = _stk.enter_context(nc.sbuf_tensor("Gbs", [128, 8 * 512], BF16))
        hgs = _stk.enter_context(nc.sbuf_tensor("hgs", [128, 8 * 512], BF16))
        r1sl = _stk.enter_context(nc.sbuf_tensor("r1sl", [128, H], F32))
        PSF = _stk.enter_context(nc.psum_tensor("PSF", [128, 3584], F32))
        PSB = _stk.enter_context(nc.psum_tensor("PSB", [128, 1024], BF16))
        s_pe = _stk.enter_context(nc.semaphore("s_pe"))
        s_act = _stk.enter_context(nc.semaphore("s_act"))
        s_dve = _stk.enter_context(nc.semaphore("s_dve"))
        s_gpc = _stk.enter_context(nc.semaphore("s_gpc"))
        s_hw = _stk.enter_context(nc.semaphore("s_hw"))
        s_sw = _stk.enter_context(nc.semaphore("s_sw"))
        block = _stk.enter_context(nc.Block())
        _body_indent_marker = None
        sems = dict(s_pe=s_pe, s_act=s_act, s_dve=s_dve, s_gpc=s_gpc,
                    s_hw=s_hw, s_sw=s_sw)
        for ci, cv in enumerate(CONSTS):
            nc.const_aps.register(F32, cv, cst[:, ci:ci + 1])
        if sc_bf:
            for cv in (0.0, 1.0):
                nc.const_aps.register(BF16, cv, cst[:, CONSTS.index(cv):
                                                   CONSTS.index(cv) + 1])

        def psA(par, n):
            return PSF[:, par * 512:par * 512 + n]

        def psP(rc, n):
            return PSF[:, 1024 + rc * 512:1024 + rc * 512 + n]

        def psDl(par, n):
            return PSF[:, 2560 + par * 512:2560 + par * 512 + n]

        def psTP(n):
            return PSF[:, 3072:3072 + n]

        def psBB(i, n):
            return PSF[:, 1024 + i * 512:1024 + i * 512 + n]

        def psXX(off, n):
            return PSF[:, 3072 + off:3072 + off + n]

        PSP_NAME = ["rgB", "rgC", "rgD"]
        PSD_NAME = ["rgE", "rgF"]
        PSBB_NAME = ["rgB", "rgC", "rgD", "rgE"]

        def psb(off, n, parts=128):
            return bass.AP(PSB.ap().tensor, off, [[1024, parts], [1, n]])

        def boot():
            for sb_t, dr_t in [(cst, consts_h), (A_sb, A_l), (cw_sb, cw_l),
                               (cb_sb, cb_l), (dtb_sb, dtb_l), (Dp_sb, Dp_l),
                               (dtln_sb, dtln_r), (bln_sb, bln_r),
                               (cln_sb, cln_r), (idb_sb, ident_b),
                               (idf_sb, ident_f), (sel_sb, sel_h)]:
                S.add("hw", lambda e, a=sb_t, b=dr_t: e.dma_start(
                    out=a[:, :], in_=b.ap()), writes=[sb_t.name])
            S.add("sw", lambda e: e.dma_start(out=wr_sb[:, :], in_=wr_l.ap()),
                  writes=["wr_sb"])
            casts = [(WinT, Wibf, H * 2 * D, 8), (WoT, Wobf, D * H, 8),
                     (AfT, Afbf, H * 1024, 2), (BfT, Bfbf, 1024 * H, 2),
                     (WxT, Wxbf, D * 288, 2), (WdtT, Wdtbf, R * D, 1)]
            for src, dst, tot, nreg in casts:
                sz = tot // nreg
                for g in range(nreg):
                    S.add("sw", lambda e, s=src, d=dst, g=g, sz=sz:
                          e.dma_start(
                              out=_v(d, g * sz, [[sz, 1], [1, sz]]),
                              in_=_v(s, g * sz, [[sz, 1], [1, sz]])),
                          writes=[f"{dst.name}_r{g}"])
            S.add("dve", lambda e: e.memset(
                _v(dAt, 0, [[N * SEG, 128], [SEG, N], [1, 1]]), 0.0),
                writes=["dAfix"])
            S.add("dve", lambda e: e.memset(
                _v(dAt, 1, [[N * SEG, 128], [SEG, N], [1, 1]]), 1.0),
                writes=["dAfix"])
            S.add("dve", lambda e: e.memset(
                _v(bbt, 1, [[N * SEG, 128], [SEG, N], [1, 1]]), 0.0),
                writes=["bbfix"])
            S.add("dve", lambda e: e.memset(carry[:, :], 0.0),
                  writes=["carry"])
            S.add("dve", lambda e: e.memset(halo[:, :], 0.0),
                  writes=[f"halo{dt}" for dt in range(NDT)])

        def ln_rows(src_t, dst_t, nrows, tag, r0=0, extra_read=None):
            nst = 4
            for tt in range(nrows // 128):
                base = (r0 + tt * 128) * H
                lt = f"{tag}lnt{tt}"
                for st4 in range(nst):
                    off = base + st4 * 1024
                    buf = xst
                    S.add("hw", lambda e, b=buf, o=off, s=src_t: e.dma_start(
                        out=b[:, :], in_=_v(s, o, [[H, 128], [1, 1024]])),
                        reads=[extra_read] if extra_read else [],
                        writes=[buf.name])
                    S.add("dve", lambda e, b=buf, s=st4: e.tensor_reduce(
                        lnt[:, s:s + 1], b[:, :], AX, ALU.add),
                        reads=[buf.name], writes=[lt])
                    S.add("act", lambda e, b=buf, s=st4: e.activation(
                        sqd[:, :], b[:, :], AF.Square),
                        reads=[buf.name], writes=["sqd"])
                    S.add("dve", lambda e, s=st4: e.tensor_reduce(
                        lnt[:, 4 + s:5 + s], sqd[:, :], AX, ALU.add),
                        reads=["sqd"], writes=[lt])
                for (o1, a1, b1) in ((0, 0, 1), (2, 2, 3), (0, 0, 2),
                                     (4, 4, 5), (6, 6, 7), (4, 4, 6)):
                    S.add("dve", lambda e, o1=o1, a1=a1, b1=b1: e.tensor_add(
                        lnt[:, o1:o1 + 1], lnt[:, a1:a1 + 1],
                        lnt[:, b1:b1 + 1]),
                        reads=[lt], writes=[lt])
                S.add("dve", lambda e: e.tensor_scalar_mul(
                    lnt[:, 8:9], lnt[:, 0:1], 1.0 / H),
                    reads=[lt], writes=[lt])
                S.add("dve", lambda e: e.tensor_scalar_mul(
                    lnt[:, 10:11], lnt[:, 4:5], 1.0 / H),
                    reads=[lt], writes=[lt])
                S.add("dve", lambda e: e.tensor_mul(
                    lnt[:, 11:12], lnt[:, 8:9], lnt[:, 8:9]),
                    reads=[lt], writes=[lt])
                S.add("dve", lambda e: e.tensor_sub(
                    lnt[:, 10:11], lnt[:, 10:11], lnt[:, 11:12]),
                    reads=[lt], writes=[lt])
                S.add("act", lambda e: e.activation(
                    lnt[:, 9:10], lnt[:, 10:11], AF.Ln, bias=1e-5),
                    reads=[lt], writes=[lt])
                S.add("act", lambda e: e.activation(
                    lnt[:, 9:10], lnt[:, 9:10], AF.Exp, scale=-0.5),
                    reads=[lt], writes=[lt])
                for st4 in range(nst):
                    off = base + st4 * 1024
                    buf = xst
                    S.add("hw", lambda e, b=buf, o=off, s=src_t: e.dma_start(
                        out=b[:, :], in_=_v(s, o, [[H, 128], [1, 1024]])),
                        reads=[lt], writes=[buf.name])
                    S.add("dve", lambda e, b=buf: e.tensor_scalar(
                        xnst[:, :], b[:, :], lnt[:, 8:9], lnt[:, 9:10],
                        ALU.subtract, ALU.mult),
                        reads=[buf.name, lt], writes=["xnst"])
                    S.add("hw", lambda e, o=off, d=dst_t: e.dma_start(
                        out=_v(d, o, [[H, 128], [1, 1024]]),
                        in_=xnst[:, :]),
                        reads=["xnst"], writes=[f"{tag}_done"])

        def proj_stage(c):
            c0, c1 = CHUNKS[c]
            tc = c1 - c0
            ndch = 64 if c == 0 else 128
            for k in range(32):
                S.add("hw", lambda e, k=k, c0=c0, tc=tc: e.dma_start_transpose(
                    _v(xn_cm, k * TC, [[32 * TC, 128], [1, tc]]),
                    _v(xn_d, c0 * H + k * 128, [[H, tc], [1, 128]])),
                    reads=["xn_done"], writes=[f"xncm{k}"])
            for dch in range(ndch):
                par = dch % 2
                wb = "wsl"
                gate = dch >= 64
                for hh in range(2):
                    src_t, rdeps, q = (
                        (WinT, [], "sw") if c == 0 else
                        (Wibf, [f"Wibf_r{g}" for g in range(8)], "hw"))
                    S.add(q, lambda e, s=src_t, dch=dch, hh=hh:
                          e.dma_start(
                              out=_v(wsl, hh * 2048,
                                     [[4096, 128], [128, 16], [1, 128]]),
                              in_=_v(s, hh * 2048 * 2 * D + dch * 128,
                                     [[2 * D, 128], [2 * D * 128, 16],
                                      [1, 128]])),
                          reads=rdeps, writes=[wb])
                for k in range(32):
                    S.add("pe", lambda e, k=k, par=par, tc=tc: e.matmul(
                        psA(par, tc),
                        _v(wsl, k * 128, [[4096, 128], [1, 128]]),
                        _v(xn_cm, k * TC, [[32 * TC, 128], [1, tc]]),
                        start=(k == 0), stop=(k == 31)),
                        reads=[wb, f"xncm{k}"], writes=[f"rgA{par}"])
                if not gate:
                    dt = dch
                    hb = f"halo{dt}"
                    S.add("act", lambda e, par=par, tc=tc: e.activation(
                        _v(uraw, 3, [[TC + 3, 128], [1, tc]]),
                        psA(par, tc), AF.Copy),
                        reads=[f"rgA{par}"], writes=["uraw"])
                    S.add("dve", lambda e, dt=dt, c=c: e.tensor_copy(
                        _v(uraw, 0, [[TC + 3, 128], [1, 3]]),
                        _v(halo, (c % 2) * NDT * 3 + dt * 3,
                           [[2 * NDT * 3, 128], [1, 3]])),
                        reads=[hb], writes=["uraw"])
                    S.add("dve", lambda e, dt=dt, c=c, tc=tc: e.tensor_copy(
                        _v(halo, ((c + 1) % 2) * NDT * 3 + dt * 3,
                           [[2 * NDT * 3, 128], [1, 3]]),
                        _v(uraw, tc, [[TC + 3, 128], [1, 3]])),
                        reads=["uraw"], writes=[hb])
                    S.add("dve", lambda e, dt=dt, tc=tc: e.tensor_scalar_mul(
                        cacc[:, :tc], _v(uraw, 0, [[TC + 3, 128], [1, tc]]),
                        cw_sb[:, dt * KC:dt * KC + 1]),
                        reads=["uraw"], writes=["cacc"])
                    for kk in range(1, KC):
                        S.add("dve", lambda e, dt=dt, kk=kk, tc=tc:
                              e.scalar_tensor_tensor(
                                  cacc[:, :tc],
                                  _v(uraw, kk, [[TC + 3, 128], [1, tc]]),
                                  cw_sb[:, dt * KC + kk:dt * KC + kk + 1],
                                  cacc[:, :tc], ALU.mult, ALU.add),
                              reads=["uraw", "cacc"], writes=["cacc"])
                    up = "up"
                    S.add("act", lambda e, dt=dt, par=par, tc=tc: e.activation(
                        _v(upost, 0, [[TC, 128], [1, tc]]),
                        cacc[:, :tc], AF.Silu, bias=cb_sb[:, dt:dt + 1]),
                        reads=["cacc"], writes=[up])
                    wxp = "wxt"
                    S.add("hw", lambda e, dt=dt, par=par: e.dma_start(
                        out=_v(wxt_sb, 0, [[288, 128], [1, 288]]),
                        in_=_v(Wxbf, dt * 128 * 288, [[288, 128], [1, 288]])),
                        reads=["Wxbf_r0", "Wxbf_r1"], writes=[wxp])
                    for rc in range(3):
                        mw = 128 if rc < 2 else 32
                        S.add("pe", lambda e, dt=dt, rc=rc, mw=mw, tc=tc,
                              par=par: e.matmul(
                                  psP(rc, tc)[0:mw, :],
                                  _v(wxt_sb, rc * 128,
                                     [[288, 128], [1, mw]]),
                                  _v(upost, 0, [[TC, 128], [1, tc]]),
                                  start=(dt == 0), stop=(dt == NDT - 1)),
                              reads=[wxp, up], writes=[PSP_NAME[rc]])
                    S.add("hw", lambda e, dt=dt, c0=c0, tc=tc, par=par:
                          e.dma_start(
                              out=_v(u_sp, dt * 128 * TOK + c0,
                                     [[TOK, 128], [1, tc]]),
                              in_=_v(upost, 0, [[TC, 128], [1, tc]])),
                          reads=[up], writes=[f"usp{dt}_{c}"])
                else:
                    gt = dch - 64
                    gp = "gs"
                    S.add("act", lambda e, par=par, tc=tc: e.activation(
                        _v(gstg, 0, [[TC, 128], [1, tc]]),
                        psA(par, tc), AF.Copy),
                        reads=[f"rgA{par}"], writes=[gp])
                    ow0 = c0 - EXT
                    S.add("hw", lambda e, gt=gt, ow0=ow0, tc=tc, par=par:
                          e.dma_start(
                              out=_v(g_sp, gt * 128 * OWN + ow0,
                                     [[OWN, 128], [1, tc]]),
                              in_=_v(gstg, 0, [[TC, 128], [1, tc]])),
                          reads=[gp], writes=[f"gsp{gt}_{c}"])

        def mid_stage(c):
            c0, c1 = CHUNKS[c]
            tc = c1 - c0
            nts = tc // 128
            for rc in range(3):
                mw = 128 if rc < 2 else 32
                S.add("act", lambda e, rc=rc, mw=mw, tc=tc: e.activation(
                    Pcme[0:mw, :tc], psP(rc, tc)[0:mw, :], AF.Copy),
                    reads=[PSP_NAME[rc]], writes=["Pcme"])
                for t_ in range(nts):
                    S.add("pe", lambda e, mw=mw, t_=t_: e.transpose(
                        psTP(mw),
                        _v(Pcme, t_ * 128, [[TC, mw], [1, 128]]),
                        idf_sb[0:mw, 0:mw]),
                        reads=["Pcme", "idf_sb"], writes=["rgF"])
                    S.add("dve", lambda e, rc=rc, mw=mw, t_=t_: e.tensor_copy(
                        _v(Ptm, t_ * 288 + rc * 128, [[4 * 288, 128], [1, mw]]),
                        psTP(mw)),
                        reads=["rgF"], writes=["Ptm"])
            for t_ in range(nts):
                po = t_ * 288
                S.add("act", lambda e, po=po: e.activation(
                    sq2[:, :], _v(Ptm, po, [[4 * 288, 128], [1, 288]]),
                    AF.Square),
                    reads=["Ptm"], writes=["sq2"])
                for (ci_, lo, n_) in ((0, 0, 256), (1, 256, 16), (2, 272, 16)):
                    S.add("dve", lambda e, ci_=ci_, lo=lo, n_=n_:
                          e.tensor_reduce(
                              lnt[:, ci_:ci_ + 1],
                              _v(sq2, lo, [[288, 128], [1, n_]]),
                              AX, ALU.add),
                          reads=["sq2"], writes=["rstat"])
                for (ci_, scale) in ((0, 1.0 / R), (1, 1.0 / N), (2, 1.0 / N)):
                    S.add("act", lambda e, ci_=ci_, scale=scale: e.activation(
                        lnt[:, 4 + ci_:5 + ci_], lnt[:, ci_:ci_ + 1],
                        AF.Ln, scale=scale, bias=1e-6),
                        reads=["rstat"], writes=["rstat"])
                    S.add("act", lambda e, ci_=ci_: e.activation(
                        lnt[:, 4 + ci_:5 + ci_], lnt[:, 4 + ci_:5 + ci_],
                        AF.Exp, scale=-0.5),
                        reads=["rstat"], writes=["rstat"])
                S.add("dve", lambda e, po=po: e.tensor_scalar_mul(
                    cacc[:, 0:256], _v(Ptm, po, [[4 * 288, 128], [1, 256]]),
                    lnt[:, 4:5]),
                    reads=["Ptm", "rstat"], writes=["cacc"])
                S.add("dve", lambda e, t_=t_: e.tensor_mul(
                    _v(dtn, t_ * 256, [[4 * 256, 128], [1, 256]]),
                    cacc[:, 0:256], dtln_sb[:, :]),
                    reads=["cacc", "dtln_sb"], writes=["dtn"])
                for wi, (lo, lcol, lw, rowb) in enumerate(
                        ((256, 5, bln_sb, 0), (272, 6, cln_sb, 32))):
                    S.add("dve", lambda e, po=po, lo=lo, lcol=lcol:
                          e.tensor_scalar_mul(
                              cacc[:, 0:16],
                              _v(Ptm, po + lo, [[4 * 288, 128], [1, 16]]),
                              lnt[:, lcol:lcol + 1]),
                          reads=["Ptm", "rstat"], writes=["cacc"])
                    S.add("dve", lambda e, lw=lw: e.tensor_mul(
                        cacc[:, 16:32], cacc[:, 0:16], lw[:, :]),
                        reads=["cacc", lw.name], writes=["cacc"])
                    S.add("dve", lambda e: e.tensor_copy(
                        bcth[:, 0:16], cacc[:, 16:32]),
                        reads=["cacc"], writes=["bcth"])
                    S.add("pe", lambda e: e.transpose(
                        psb(0, 128, 16), bcth[:, 0:16], idb_sb[:, :]),
                        reads=["bcth", "idb_sb"], writes=["rgS"])
                    S.add("dve", lambda e, rowb=rowb, t_=t_: e.tensor_copy(
                        _v(bct, rowb * TC + t_ * 128, [[TC, 16], [1, 128]]),
                        psb(0, 128, 16)),
                        reads=["rgS"], writes=["bct"])
            for t_ in range(nts):
                for rt in range(2):
                    S.add("pe", lambda e, t_=t_, rt=rt: e.transpose(
                        psb(256, 128), _v(dtn, t_ * 256 + rt * 128,
                                          [[4 * 256, 128], [1, 128]]),
                        idb_sb[:, :]),
                        reads=["dtn", "idb_sb"], writes=["rgS"])
                    S.add("dve", lambda e, t_=t_, rt=rt: e.tensor_copy(
                        _v(dtnT, rt * TC + t_ * 128,
                           [[2 * TC, 128], [1, 128]]),
                        psb(256, 128)),
                        reads=["rgS"], writes=["dtnT"])
            S.add("sw", lambda e, c=c, tc=tc: e.dma_start(
                out=_v(bc_d, (c % 2) * 64 * TC, [[TC, 64], [1, tc]]),
                in_=_v(bct, 0, [[TC, 64], [1, tc]])),
                reads=["bct"], writes=["bc_dram"])

        def delta_stage(c):
            c0, c1 = CHUNKS[c]
            tc = c1 - c0
            par_c = c % 2
            S.add("hw", lambda e, c0=c0, tc=tc: e.dma_start(
                out=mask_sb[:, :tc], in_=_v(maskr, c0, [[0, 128], [1, tc]])),
                writes=["mask_sb"])
            for dtg in range(8):
                wp = "wdt"
                S.add("hw", lambda e, dtg=dtg: e.dma_start(
                    out=_v(wdt_sb, 0,
                           [[2048, 128], [1024, 2], [1, 1024]]),
                    in_=_v(Wdtbf, dtg * 1024,
                           [[D, 128], [128 * D, 2], [1, 1024]])),
                    reads=["Wdtbf_r0"], writes=[wp])
                for dts in range(DTG):
                    dt = dtg * DTG + dts
                    pp = dt % 2
                    for rt in range(2):
                        S.add("pe", lambda e, dtg=dtg, dts=dts, rt=rt, tc=tc,
                              pp=pp: e.matmul(
                                  psDl(pp, tc),
                                  _v(wdt_sb, rt * 1024 + dts * 128,
                                     [[2048, 128], [1, 128]]),
                                  _v(dtnT, rt * TC, [[2 * TC, 128], [1, tc]]),
                                  start=(rt == 0), stop=(rt == 1)),
                              reads=[wp, "dtnT"], writes=[PSD_NAME[pp]])
                    S.add("dve", lambda e, dt=dt, pp=pp, tc=tc:
                          e.tensor_scalar_add(
                              ztl[:, :tc], psDl(pp, tc),
                              dtb_sb[:, dt:dt + 1]),
                          reads=[PSD_NAME[pp]], writes=["ztl"])
                    S.add("act", lambda e, tc=tc: e.activation(
                        zab[:, :tc], ztl[:, :tc], AF.Abs),
                        reads=["ztl"], writes=["zab"])
                    S.add("act", lambda e, tc=tc: e.activation(
                        zab[:, :tc], zab[:, :tc], AF.Exp, scale=-1.0),
                        reads=["zab"], writes=["zab"])
                    S.add("act", lambda e, tc=tc: e.activation(
                        zab[:, :tc], zab[:, :tc], AF.Ln, bias=1.0),
                        reads=["zab"], writes=["zab"])
                    S.add("act", lambda e, tc=tc: e.activation(
                        ztl[:, :tc], ztl[:, :tc], AF.Relu),
                        reads=["ztl"], writes=["ztl"])
                    S.add("dve", lambda e, tc=tc: e.tensor_add(
                        ztl[:, :tc], ztl[:, :tc], zab[:, :tc]),
                        reads=["ztl", "zab"], writes=["ztl"])
                    S.add("hw", lambda e, dt=dt, par_c=par_c, tc=tc:
                          e.dma_start(
                              out=_v(del_d, (par_c * D + dt * 128) * TC,
                                     [[TC, 128], [1, tc]]),
                              in_=ztl[:, :tc]),
                          reads=["ztl"], writes=[f"deld{dt}"])
                    S.add("hw", lambda e, dt=dt, c0=c0, tc=tc: e.dma_start(
                        out=uslD[:, :tc],
                        in_=_v(u_sp, dt * 128 * TOK + c0,
                               [[TOK, 128], [1, tc]])),
                        reads=[f"usp{dt}_{c}"], writes=["uslD"])
                    S.add("dve", lambda e, tc=tc: e.tensor_mul(
                        zab[:, :tc], ztl[:, :tc], mask_sb[:, :tc]),
                        reads=["ztl", "mask_sb"], writes=["zab"])
                    S.add("dve", lambda e, tc=tc: e.tensor_mul(
                        zab[:, :tc], zab[:, :tc], uslD[:, :tc]),
                        reads=["zab", "uslD"], writes=["zab"])
                    if sc_bf:
                        S.add("dve", lambda e, tc=tc: e.tensor_copy(
                            dutb[:, :tc], zab[:, :tc]),
                            reads=["zab"], writes=["dutb"])
                        dub, dur = dutb, "dutb"
                    else:
                        dub, dur = zab, "zab"
                    S.add("hw", lambda e, dt=dt, par_c=par_c, tc=tc, b=dub:
                          e.dma_start(
                              out=_v(du_d, (par_c * D + dt * 128) * TC,
                                     [[TC, 128], [1, tc]]),
                              in_=b[:, :tc]),
                          reads=[dur], writes=[f"dud{dt}"])

        def scan_stage(c):
            c0, c1 = CHUNKS[c]
            tc = c1 - c0
            nw = tc // TS
            par_c = c % 2
            ow0 = c0 - EXT
            owned = c > 0
            pool = "gpc" if cfg["pool_offload"] else "dve"
            for w in range(nw):
                bq = "sw" if sc_bf else "hw"
                for bcsb, rowb, nm in ((bcB, 0, "bcB"), (bcC, 32, "bcC")):
                    S.add(bq, lambda e, b=bcsb, rowb=rowb, w=w, par_c=par_c:
                          e.dma_start(
                              out=_v(b, 0, [[N * TS, 128], [TS, N], [1, TS]]),
                              in_=_v(bc_d, (par_c * 64 + rowb) * TC + w * TS,
                                     [[0, 128], [TC, N], [1, TS]])),
                          reads=["bc_dram"], writes=[nm])
                for dtg in range(8):
                    gp = 0
                    go = 0
                    base_r = dtg * DTG * 128
                    S.add("hw", lambda e, go=go, base_r=base_r, w=w,
                          par_c=par_c: e.dma_start(
                              out=_v(dsl, go, [[DTG * TS, 128],
                                               [TS, DTG], [1, TS]]),
                              in_=_v(del_d, (par_c * D + base_r) * TC + w * TS,
                                     [[TC, 128], [128 * TC, DTG], [1, TS]])),
                          reads=[f"deld{dtg * DTG + i}" for i in range(DTG)],
                          writes=[f"dsl{gp}"])
                    S.add("hw", lambda e, go=go, base_r=base_r, w=w,
                          par_c=par_c: e.dma_start(
                              out=_v(dusl, go, [[DTG * TS, 128],
                                                [TS, DTG], [1, TS]]),
                              in_=_v(du_d, (par_c * D + base_r) * TC + w * TS,
                                     [[TC, 128], [128 * TC, DTG], [1, TS]])),
                          reads=[f"dud{dtg * DTG + i}" for i in range(DTG)],
                          writes=[f"dusl{gp}"])
                    if owned:
                        S.add("hw", lambda e, go=go, base_r=base_r, w=w,
                              c0=c0: e.dma_start(
                                  out=_v(usl, go, [[DTG * TS, 128],
                                                   [TS, DTG], [1, TS]]),
                                  in_=_v(u_sp, base_r * TOK + c0 + w * TS,
                                         [[TOK, 128], [128 * TOK, DTG],
                                          [1, TS]])),
                              reads=[f"usp{dtg * DTG + i}_{c}"
                                     for i in range(DTG)],
                              writes=[f"usl{gp}"])
                        S.add("hw", lambda e, go=go, base_r=base_r, w=w,
                              ow0=ow0, c=c: e.dma_start(
                                  out=_v(gsl, go, [[DTG * TS, 128],
                                                   [TS, DTG], [1, TS]]),
                                  in_=_v(g_sp, base_r * OWN + ow0 + w * TS,
                                         [[OWN, 128], [128 * OWN, DTG],
                                          [1, TS]])),
                              reads=[f"gsp{dtg * DTG + i}_{c}"
                                     for i in range(DTG)],
                              writes=[f"gsl{gp}"])
                    for dts in range(DTG):
                        dt = dtg * DTG + dts
                        sp = 0
                        dab = "dA0"
                        stb = "st0"
                        dof = 0
                        slo = go + dts * TS
                        for n in range(N):
                            S.add("act", lambda e, n=n, dof=dof, slo=slo,
                                  dt=dt: e.activation(
                                      _v(dAt, dof + n * SEG + 2,
                                         [[N * SEG, 128], [1, TS]]),
                                      _v(dsl, slo, [[DTG * TS, 128],
                                                    [1, TS]]),
                                      AF.Exp,
                                      scale=A_sb[:, dt * N + n:dt * N + n + 1]),
                                  reads=[f"dsl{gp}", "dAfix", "A_sb"],
                                  writes=[dab])
                        S.add("dve", lambda e, dt=dt: e.tensor_copy(
                            _v(bbt, 0, [[N * SEG, 128], [SEG, N]]),
                            _v(carry, dt * N, [[NDT * N, 128], [1, N]])),
                            reads=["carry", "bbfix"], writes=["bbt"])
                        S.add("dve", lambda e, slo=slo: e.tensor_tensor(
                            _v(bbt, 2, [[N * SEG, 128], [SEG, N], [1, TS]]),
                            _v(dusl, slo, [[DTG * TS, 128], [0, N],
                                           [1, TS]]),
                            _v(bcB, 0, [[N * TS, 128], [TS, N], [1, TS]]),
                            ALU.mult),
                            reads=[f"dusl{gp}", "bcB"], writes=["bbt"])
                        S.add("dve", lambda e, dof=dof, sp=sp:
                              e.tensor_tensor_scan(
                                  _v(stt, 0,
                                     [[N * SEG, 128], [1, N * SEG]]),
                                  _v(dAt, dof,
                                     [[N * SEG, 128], [1, N * SEG]]),
                                  _v(bbt, 0, [[N * SEG, 128], [1, N * SEG]]),
                                  0.0, ALU.mult, ALU.add),
                              reads=[dab, "bbt"], writes=[stb])
                        S.add("dve", lambda e, dt=dt, sp=sp: e.tensor_copy(
                            _v(carry, dt * N, [[NDT * N, 128], [1, N]]),
                            _v(stt, SEG - 1,
                               [[N * SEG, 128], [SEG, N]])),
                            reads=[stb], writes=["carry"])
                        if owned:
                            S.add(pool, lambda e, dof=dof, sp=sp:
                                  e.tensor_tensor(
                                      _v(dAt, dof + 2,
                                         [[N * SEG, 128], [SEG, N],
                                          [1, TS]]),
                                      _v(stt, 2,
                                         [[N * SEG, 128], [SEG, N],
                                          [1, TS]]),
                                      _v(bcC, 0, [[N * TS, 128], [TS, N],
                                                  [1, TS]]),
                                      ALU.mult),
                                  reads=[stb, "bcC"], writes=[dab])
                            S.add(pool, lambda e, dof=dof: e.tensor_add(
                                _v(dAt, dof + 2,
                                   [[N * SEG, 128], [SEG, 8], [1, TS]]),
                                _v(dAt, dof + 2,
                                   [[N * SEG, 128], [SEG, 8], [1, TS]]),
                                _v(dAt, dof + 2 + 8 * SEG,
                                   [[N * SEG, 128], [SEG, 8], [1, TS]])),
                                reads=[dab], writes=[dab])
                            S.add("dve", lambda e, dof=dof: e.tensor_add(
                                _v(dAt, dof + 2,
                                   [[N * SEG, 128], [SEG, 4], [1, TS]]),
                                _v(dAt, dof + 2,
                                   [[N * SEG, 128], [SEG, 4], [1, TS]]),
                                _v(dAt, dof + 2 + 4 * SEG,
                                   [[N * SEG, 128], [SEG, 4], [1, TS]])),
                                reads=[dab], writes=[dab])
                            S.add("dve", lambda e, dof=dof: e.tensor_add(
                                _v(dAt, dof + 2,
                                   [[N * SEG, 128], [SEG, 2], [1, TS]]),
                                _v(dAt, dof + 2,
                                   [[N * SEG, 128], [SEG, 2], [1, TS]]),
                                _v(dAt, dof + 2 + 2 * SEG,
                                   [[N * SEG, 128], [SEG, 2], [1, TS]])),
                                reads=[dab], writes=[dab])
                            S.add("dve", lambda e, dof=dof: e.tensor_add(
                                ytmp[:, :],
                                _v(dAt, dof + 2, [[N * SEG, 128],
                                                  [1, TS]]),
                                _v(dAt, dof + 2 + SEG, [[N * SEG, 128],
                                                        [1, TS]])),
                                reads=[dab], writes=["ytmp"])
                            S.add("act", lambda e, slo=slo: e.activation(
                                sgt[:, :],
                                _v(gsl, slo, [[DTG * TS, 128], [1, TS]]),
                                AF.Silu),
                                reads=[f"gsl{gp}"], writes=["sgt"])
                            S.add("dve", lambda e, slo=slo, dt=dt:
                                  e.scalar_tensor_tensor(
                                      z1t[:, :],
                                      _v(usl, slo, [[DTG * TS, 128],
                                                    [1, TS]]),
                                      Dp_sb[:, dt:dt + 1], ytmp[:, :],
                                      ALU.mult, ALU.add),
                                  reads=[f"usl{gp}", "ytmp", "Dp_sb"],
                                  writes=["z1t"])
                            S.add("dve", lambda e, slo=slo: e.tensor_mul(
                                _v(ysl, slo, [[DTG * TS, 128], [1, TS]]),
                                z1t[:, :], sgt[:, :]),
                                reads=["z1t", "sgt"], writes=[f"ysl{gp}"])
                    if owned:
                        S.add("hw", lambda e, go=go, base_r=base_r, ow0=ow0,
                              w=w: e.dma_start(
                                  out=_v(y_sp, base_r * OWN + ow0 + w * TS,
                                         [[OWN, 128], [128 * OWN, DTG],
                                          [1, TS]]),
                                  in_=_v(ysl, go, [[DTG * TS, 128],
                                                   [TS, DTG], [1, TS]])),
                              reads=[f"ysl{gp}"],
                              writes=[f"yspw{c}_{w}_{dtg}"])

        def phaseB(c):
            c0, c1 = CHUNKS[c]
            ow0 = c0 - EXT
            ydeps = [f"yspw{c}_{w}_{g}" for w in range(TC // TS)
                     for g in range(8)]
            for hc in range(8):
                S.add("hw", lambda e, hc=hc, ow0=ow0: e.dma_start(
                    out=xsl1[:, :],
                    in_=_v(x_ext, (EXT + ow0) * H + hc * 512,
                           [[H, 128], [128 * H, 4], [1, 512]])),
                    writes=["xsl1"])
                for kg in range(16):
                    if c == 1:
                        S.add("sw", lambda e, hc=hc, kg=kg: e.dma_start(
                            out=wB1[:, :],
                            in_=_v(WoT, kg * 512 * H + hc * 512,
                                   [[H, 128], [128 * H, 4], [1, 512]])),
                            writes=["wB1"])
                    else:
                        S.add("hw", lambda e, hc=hc, kg=kg: e.dma_start(
                            out=wB1[:, :],
                            in_=_v(Wobf, kg * 512 * H + hc * 512,
                                   [[H, 128], [128 * H, 4], [1, 512]])),
                            reads=[f"Wobf_r{g}" for g in range(8)],
                            writes=["wB1"])
                    S.add("hw", lambda e, kg=kg, ow0=ow0: e.dma_start(
                        out=ysb1[:, :],
                        in_=_v(y_sp, kg * 512 * OWN + ow0,
                               [[OWN, 128], [128 * OWN, 4], [1, 512]])),
                        reads=ydeps, writes=["ysb1"])
                    for kk in range(4):
                        for tt in range(4):
                            S.add("pe", lambda e, kk=kk, tt=tt, kg=kg:
                                  e.matmul(
                                      psBB(tt, 512),
                                      _v(ysb1, kk * 512 + tt * 128,
                                         [[4 * 512, 128], [1, 128]]),
                                      _v(wB1, kk * 512,
                                         [[4 * 512, 128], [1, 512]]),
                                      start=(kg == 0 and kk == 0),
                                      stop=(kg == 15 and kk == 3)),
                                  reads=["wB1", "ysb1"],
                                  writes=[PSBB_NAME[tt]])
                for tt in range(4):
                    S.add("dve", lambda e, tt=tt, hc=hc: e.tensor_add(
                        _v(xsl1, tt * 512, [[4 * 512, 128], [1, 512]]),
                        psBB(tt, 512),
                        _v(xsl1, tt * 512, [[4 * 512, 128], [1, 512]])),
                        reads=[PSBB_NAME[tt], "xsl1"], writes=["xsl1"])
                S.add("hw", lambda e, hc=hc, ow0=ow0: e.dma_start(
                    out=_v(r1_d, ow0 * H + hc * 512,
                           [[H, 128], [128 * H, 4], [1, 512]]),
                    in_=_v(xsl1, 0, [[4 * 512, 128], [512, 4], [1, 512]])),
                    reads=["xsl1"], writes=["r1w"])
            ln_rows(r1_d, xn2_d, 512, f"x2_{c}", r0=ow0,
                    extra_read="r1w")
            for k in range(32):
                kp = k % 2
                S.add("hw", lambda e, k=k, kp=kp, ow0=ow0:
                      e.dma_start_transpose(
                          _v(x2T, kp * 512, [[2 * 512, 128], [1, 512]]),
                          _v(xn2_d, ow0 * H + k * 128, [[H, 512], [1, 128]])),
                      reads=[f"x2_{c}_done"], writes=[f"x2T{kp}"])
                for tt in range(4):
                    S.add("pe", lambda e, k=k, kp=kp, tt=tt: e.matmul(
                        psBB(tt, 64),
                        _v(x2T, kp * 512 + tt * 128,
                           [[2 * 512, 128], [1, 128]]),
                        _v(wr_sb, k * 64, [[32 * 64, 128], [1, 64]]),
                        start=(k == 0), stop=(k == 31)),
                        reads=[f"x2T{kp}", "wr_sb"], writes=[PSBB_NAME[tt]])
            for tt in range(4):
                S.add("dve", lambda e, tt=tt: e.tensor_copy(
                    sco[:, :], psBB(tt, 64)),
                    reads=[PSBB_NAME[tt]], writes=["sco"])
                S.add("dve", lambda e: e.tensor_copy(scw[:, :], sco[:, :]),
                      reads=["sco"], writes=["scw"])
                S.add("dve", lambda e: e.tensor_reduce(
                    mcol[:, 0:1], scw[:, :], AX, ALU.max),
                    reads=["scw"], writes=["mcol"])
                S.add("dve", lambda e: e.tensor_copy(
                    mcol[:, 1:2], mcol[:, 0:1]),
                    reads=["mcol"], writes=["mcol"])
                for it in range(15):
                    S.add("dve", lambda e: e.tensor_single_scalar(
                        gels[:, :], scw[:, :], mcol[:, 1:2], ALU.is_ge),
                        reads=["scw", "mcol"], writes=["gels"])
                    S.add("dve", lambda e: e.scalar_tensor_tensor(
                        scw[:, :], gels[:, :], BIGNEG, scw[:, :],
                        ALU.mult, ALU.add),
                        reads=["gels", "scw"], writes=["scw"])
                    S.add("dve", lambda e: e.tensor_reduce(
                        mcol[:, 1:2], scw[:, :], AX, ALU.max),
                        reads=["scw"], writes=["mcol"])
                S.add("dve", lambda e: e.tensor_single_scalar(
                    gels[:, :], sco[:, :], mcol[:, 1:2], ALU.is_ge),
                    reads=["sco", "mcol"], writes=["gels"])
                S.add("dve", lambda e: e.tensor_scalar_mul(
                    mcol[:, 2:3], mcol[:, 0:1], -1.0),
                    reads=["mcol"], writes=["mcol"])
                S.add("act", lambda e: e.activation(
                    Gf[:, :], sco[:, :], AF.Exp, bias=mcol[:, 2:3]),
                    reads=["sco", "mcol"], writes=["Gf"])
                S.add("dve", lambda e: e.tensor_mul(
                    Gf[:, :], Gf[:, :], gels[:, :]),
                    reads=["Gf", "gels"], writes=["Gf"])
                S.add("dve", lambda e: e.tensor_reduce(
                    mcol[:, 3:4], Gf[:, :], AX, ALU.add),
                    reads=["Gf"], writes=["mcol"])
                S.add("dve", lambda e: e.reciprocal(
                    mcol[:, 3:4], mcol[:, 3:4]),
                    reads=["mcol"], writes=["mcol"])
                S.add("dve", lambda e: e.tensor_scalar(
                    Gb16[:, :], Gf[:, :], mcol[:, 3:4], 2.0,
                    ALU.mult, ALU.mult),
                    reads=["Gf", "mcol"], writes=["Gb16"])
                if cfg["debug"] and c == len(CHUNKS) - 1 and tt == 0:
                    S.add("sw", lambda e: e.dma_start(
                        out=_v(dbg["tk"], 0, [[320, 128], [1, 64]]),
                        in_=scw[:, :]), reads=["scw"], writes=["dbgtk"])
                    S.add("sw", lambda e: e.dma_start(
                        out=_v(dbg["tk"], 64, [[320, 128], [1, 64]]),
                        in_=gels[:, :]), reads=["gels"], writes=["dbgtk"])
                    S.add("sw", lambda e: e.dma_start(
                        out=_v(dbg["tk"], 128, [[320, 128], [1, 64]]),
                        in_=Gf[:, :]), reads=["Gf"], writes=["dbgtk"])
                    S.add("sw", lambda e: e.dma_start(
                        out=_v(dbg["tk"], 192, [[320, 128], [1, 4]]),
                        in_=mcol[:, :]), reads=["mcol"], writes=["dbgtk"])
                    S.add("sw", lambda e: e.dma_start(
                        out=_v(dbg["tk"], 200, [[320, 128], [1, 64]]),
                        in_=sco[:, :]), reads=["sco"], writes=["dbgtk"])
                S.add("pe", lambda e: e.transpose(
                    psb(512, 128, 64), Gb16[:, :], idb_sb[:, :]),
                    reads=["Gb16", "idb_sb"], writes=["rgS"])
                S.add("dve", lambda e, tt=tt: e.tensor_copy(
                    _v(GT, tt * 128, [[512, 64], [1, 128]]),
                    psb(512, 128, 64)),
                    reads=["rgS"], writes=["GT"])
            for kr in range(8):
                S.add("pe", lambda e, kr=kr: e.matmul(
                    psXX(0, 512),
                    _v(sel_sb, kr * 128, [[8 * 128, 64], [1, 128]]),
                    _v(GT, 0, [[512, 64], [1, 512]]),
                    start=True, stop=True),
                    reads=["GT", "sel_sb"], writes=["rgF"])
                S.add("act", lambda e, kr=kr: e.activation(
                    _v(Gbs, kr * 512, [[8 * 512, 128], [1, 512]]),
                    psXX(0, 512), AF.Copy),
                    reads=["rgF"], writes=["Gbs"])
            for rnd in range(2):
                for k in range(32):
                    kp = k % 2
                    S.add("hw", lambda e, k=k, kp=kp, ow0=ow0:
                          e.dma_start_transpose(
                              _v(x2T, kp * 512, [[2 * 512, 128], [1, 512]]),
                              _v(xn2_d, ow0 * H + k * 128,
                                 [[H, 512], [1, 128]])),
                          reads=[f"x2_{c}_done"], writes=[f"x2T{kp}"])
                    ap_ = f"afs{kp}"
                    if c == 1:
                        S.add("sw", lambda e, k=k, rnd=rnd, kp=kp:
                              e.dma_start(
                                  out=_v(afs, kp * 512,
                                         [[2 * 512, 128], [1, 512]]),
                                  in_=_v(AfT, k * 128 * 1024 + rnd * 512,
                                         [[1024, 128], [1, 512]])),
                              writes=[ap_])
                    else:
                        S.add("hw", lambda e, k=k, rnd=rnd, kp=kp:
                              e.dma_start(
                                  out=_v(afs, kp * 512,
                                         [[2 * 512, 128], [1, 512]]),
                                  in_=_v(Afbf, k * 128 * 1024 + rnd * 512,
                                         [[1024, 128], [1, 512]])),
                              reads=["Afbf_r0", "Afbf_r1"], writes=[ap_])
                    for kri in range(4):
                        S.add("pe", lambda e, k=k, kri=kri, kp=kp: e.matmul(
                            psBB(kri, 512),
                            _v(afs, kp * 512 + kri * 128,
                               [[2 * 512, 128], [1, 128]]),
                            _v(x2T, kp * 512, [[2 * 512, 128], [1, 512]]),
                            start=(k == 0), stop=(k == 31)),
                            reads=[ap_, f"x2T{kp}"], writes=[PSBB_NAME[kri]])
                for kri in range(4):
                    kr = rnd * 4 + kri
                    S.add("act", lambda e, kr=kr, kri=kri: e.activation(
                        _v(hgs, kr * 512, [[8 * 512, 128], [1, 512]]),
                        psBB(kri, 512), AF.Copy),
                        reads=[PSBB_NAME[kri]], writes=["hgs"])
            S.add("dve", lambda e: e.tensor_mul(
                hgs[:, :], hgs[:, :], Gbs[:, :]),
                reads=["hgs", "Gbs"], writes=["hgs"])
            for tsub in range(4):
                S.add("hw", lambda e, tsub=tsub, ow0=ow0: e.dma_start(
                    out=r1sl[:, :],
                    in_=_v(r1_d, (ow0 + tsub * 128) * H,
                           [[H, 128], [1, H]])),
                    reads=["r1w"], writes=["r1sl"])
                for half in range(2):
                    for hcq in range(4):
                        hc = half * 4 + hcq
                        for kh in range(2):
                            if c == 1:
                                S.add("sw", lambda e, hc=hc, kh=kh:
                                      e.dma_start(
                                          out=wB1[:, :],
                                          in_=_v(BfT, kh * 512 * H + hc * 512,
                                                 [[H, 128], [128 * H, 4],
                                                  [1, 512]])),
                                      writes=["wB1"])
                            else:
                                S.add("hw", lambda e, hc=hc, kh=kh:
                                      e.dma_start(
                                          out=wB1[:, :],
                                          in_=_v(Bfbf, kh * 512 * H + hc * 512,
                                                 [[H, 128], [128 * H, 4],
                                                  [1, 512]])),
                                      reads=["Bfbf_r0", "Bfbf_r1"],
                                      writes=["wB1"])
                            for kk in range(4):
                                kr = kh * 4 + kk
                                S.add("pe", lambda e, kr=kr, kk=kk, hcq=hcq,
                                      tsub=tsub: e.matmul(
                                          psBB(hcq, 512),
                                          _v(hgs, kr * 512 + tsub * 128,
                                             [[8 * 512, 128], [1, 128]]),
                                          _v(wB1, kk * 512,
                                             [[4 * 512, 128], [1, 512]]),
                                          start=(kr == 0), stop=(kr == 7)),
                                      reads=["wB1", "hgs"],
                                      writes=[PSBB_NAME[hcq]])
                    for hcq in range(4):
                        hc = half * 4 + hcq
                        S.add("dve", lambda e, hcq=hcq, hc=hc: e.tensor_add(
                            _v(r1sl, hc * 512, [[H, 128], [1, 512]]),
                            psBB(hcq, 512),
                            _v(r1sl, hc * 512, [[H, 128], [1, 512]])),
                            reads=[PSBB_NAME[hcq], "r1sl"], writes=["r1sl"])
                S.add("hw", lambda e, tsub=tsub, ow0=ow0: e.dma_start(
                    out=_v(out_own, (ow0 + tsub * 128) * H,
                           [[H, 128], [1, H]]),
                    in_=r1sl[:, :]),
                    reads=["r1sl"], writes=["outw"])

        stop = cfg.get("stop_after", "full")
        order = ["ln", "proj", "mid", "delta", "scan", "full"]

        def on(st):
            return order.index(st) <= order.index(stop)
        boot()
        ln_rows(x_ext, xn_d, TOK, "xn")
        nch = cfg.get("nchunks", len(CHUNKS))
        for c in range(nch):
            if on("proj"):
                proj_stage(c)
            if on("mid"):
                mid_stage(c)
            if on("delta"):
                delta_stage(c)
            if c >= 2 and stop == "full":
                phaseB(c - 1)
            if on("scan"):
                scan_stage(c)
        if stop == "full" and nch == len(CHUNKS):
            phaseB(len(CHUNKS) - 1)
        if cfg["debug"]:
            S.add("sw", lambda e: e.dma_start(out=dbg["u"].ap(),
                                              in_=u_sp.ap()),
                  reads=[f"usp{dt}_{c}" for dt in range(NDT)
                         for c in range(len(CHUNKS))], writes=["dbgu"])
            S.add("sw", lambda e: e.dma_start(out=dbg["y"].ap(),
                                              in_=y_sp.ap()),
                  reads=[f"yspw{c}_{w}_{g}" for c in range(1, len(CHUNKS))
                         for w in range(TC // TS) for g in range(8)],
                  writes=["dbgy"])
            S.add("sw", lambda e: e.dma_start(out=dbg["xn"].ap(),
                                              in_=xn_d.ap()),
                  reads=["xn_done"], writes=["dbgxn"])
            S.add("sw", lambda e: e.dma_start(out=dbg["r1"].ap(),
                                              in_=r1_d.ap()),
                  reads=["r1w"], writes=["dbgr1"])
            S.add("sw", lambda e: e.dma_start(out=dbg["del"].ap(),
                                              in_=del_d.ap()),
                  reads=[f"deld{dt}" for dt in range(NDT)],
                  writes=["dbgdel"])
            S.add("sw", lambda e: e.dma_start(out=dbg["xn2"].ap(),
                                              in_=xn2_d.ap()),
                  reads=[f"x2_{len(CHUNKS) - 1}_done"], writes=["dbgxn2"])
            S.add("sw", lambda e: e.dma_start(out=dbg["gbs"].ap(),
                                              in_=Gbs[:, :]),
                  reads=["Gbs"], writes=["dbggbs"])
            S.add("sw", lambda e: e.dma_start(out=dbg["hgs"].ap(),
                                              in_=hgs[:, :]),
                  reads=["hgs"], writes=["dbghgs"])
            S.add("sw", lambda e: e.dma_start(out=dbg["gt"].ap(),
                                              in_=GT[:, :]),
                  reads=["GT"], writes=["dbggt"])
            S.add("sw", lambda e: e.dma_start(out=dbg["sco"].ap(),
                                              in_=sco[:, :]),
                  reads=["sco"], writes=["dbgsco"])
        S.emit(block, sems)
    return nc


def _host_layout(inputs):
    f32 = np.float32
    bf = ml_dtypes.bfloat16
    x = np.asarray(inputs["L0_attn_hidden_states"], f32).reshape(2 * 8192, H)
    WinT = np.ascontiguousarray(np.asarray(inputs["L0_attn_W_in"], f32).T)
    WxT = np.ascontiguousarray(np.asarray(inputs["L0_attn_Wx"], f32).T)
    WdtT = np.ascontiguousarray(np.asarray(inputs["L0_attn_Wdt"], f32).T)
    WoT = np.ascontiguousarray(np.asarray(inputs["L0_attn_W_out"], f32).T)
    Af = np.asarray(inputs["L0_ffn_A"], f32).reshape(1024, H)
    AfT = np.ascontiguousarray(Af.T)
    BfT = np.ascontiguousarray(
        np.asarray(inputs["L0_ffn_B"], f32).transpose(0, 2, 1).reshape(1024, H))
    Wr = np.asarray(inputs["L0_ffn_Wr"], f32)
    wr_l = np.ascontiguousarray(
        Wr.T.reshape(32, 128, 64).transpose(1, 0, 2).reshape(128, 32 * 64))
    A_l = np.ascontiguousarray(np.asarray(inputs["L0_attn_A"], f32)
                               .reshape(NDT, 128, N).transpose(1, 0, 2)
                               .reshape(128, NDT * N))
    cw_l = np.ascontiguousarray(np.asarray(inputs["L0_attn_conv_w"], f32)
                                .reshape(NDT, 128, KC).transpose(1, 0, 2)
                                .reshape(128, NDT * KC))
    cb_l = np.ascontiguousarray(np.asarray(inputs["L0_attn_conv_b"], f32)
                                .reshape(NDT, 128).T)
    dtb_l = np.ascontiguousarray(np.asarray(inputs["L0_attn_dt_bias"], f32)
                                 .reshape(NDT, 128).T)
    Dp_l = np.ascontiguousarray(np.asarray(inputs["L0_attn_D_param"], f32)
                                .reshape(NDT, 128).T)
    dtln_r = np.ascontiguousarray(np.broadcast_to(
        np.asarray(inputs["L0_attn_dt_ln_w"], f32), (128, R)))
    bln_r = np.ascontiguousarray(np.broadcast_to(
        np.asarray(inputs["L0_attn_b_ln_w"], f32), (128, N)))
    cln_r = np.ascontiguousarray(np.broadcast_to(
        np.asarray(inputs["L0_attn_c_ln_w"], f32), (128, N)))
    consts = np.ascontiguousarray(np.broadcast_to(
        np.array(CONSTS, f32), (128, len(CONSTS))))
    ident_f = np.eye(128, dtype=f32)
    ident_b = ident_f.astype(bf)
    sel = np.zeros((64, 8, 128), f32)
    for kt in range(8):
        for g in range(8):
            sel[8 * kt + g, kt, g * 16:(g + 1) * 16] = 1.0
    sel_h = np.ascontiguousarray(sel.reshape(64, 8 * 128)).astype(bf)

    shared = dict(WinT=WinT, WxT=WxT, WdtT=WdtT, WoT=WoT, AfT=AfT, BfT=BfT,
                  wr_l=wr_l, A_l=A_l, cw_l=cw_l, cb_l=cb_l, dtb_l=dtb_l,
                  Dp_l=Dp_l, dtln_r=dtln_r, bln_r=bln_r, cln_r=cln_r,
                  consts_h=consts, ident_b=ident_b, ident_f=ident_f,
                  sel_h=sel_h)
    in_maps = []
    for core in range(NC_CORES):
        b, s = core // 4, core % 4
        r0 = b * 8192 + s * OWN
        if s == 0:
            x_e = np.concatenate(
                [np.zeros((EXT, H), f32), x[r0:r0 + OWN]], axis=0)
            mask = np.concatenate([np.zeros(EXT, f32), np.ones(OWN, f32)])
        else:
            x_e = x[r0 - EXT:r0 + OWN]
            mask = np.ones(TOK, f32)
        m = dict(shared)
        m["x_ext"] = np.ascontiguousarray(x_e)
        m["maskr"] = mask.reshape(1, TOK)
        in_maps.append(m)
    return in_maps


def kernel(**inputs):
    in_maps = _host_layout(inputs)
    nc = build_program(CFG)
    res = run_bass_kernel_spmd(nc, in_maps, core_ids=list(range(NC_CORES)))
    outs = [np.asarray(res.results[i]["out_own"]) for i in range(NC_CORES)]
    full = np.concatenate(outs, axis=0).reshape(2, 8192, H)
    return full
